# version 01 (final)
# baseline (speedup 1.0000x reference)
import os
import math
import numpy as np
import ml_dtypes
from contextlib import ExitStack
import concourse.bass as bass
import concourse.mybir as mybir
from concourse.bass_utils import run_bass_kernel_spmd

F32 = mybir.dt.float32
BF16 = mybir.dt.bfloat16
AF = mybir.ActivationFunctionType
ALU = mybir.AluOpType
AX = mybir.AxisListType
BF = ml_dtypes.bfloat16

NEG = -1e30
EPS = 1e-5

LAST_DEBUG = {}


class Prog:

    def __init__(self, nc):
        self.nc = nc
        self.ops = []

    def op(self, eng, fn, inc=1):
        self.ops.append((eng, fn, inc))

    def dma(self, out, in_, eng="sync", **kw):
        self.ops.append((eng, lambda e: e.dma_start(out=out, in_=in_, **kw), 16))

    def build(self):
        nc = self.nc
        before = []
        tot = 0
        for (_, _, inc) in self.ops:
            before.append(tot)
            tot += inc
        self.total = tot
        with nc.semaphore("tick") as TICK:
            with nc.Block() as block:
                for ename in ["sync", "tensor", "vector", "scalar", "gpsimd"]:
                    myops = [(i, fn) for i, (e, fn, _) in enumerate(self.ops) if e == ename]
                    if not myops and ename != "gpsimd":
                        continue

                    def mk(myops, ename):
                        def section(eng):
                            for i, fn in myops:
                                if before[i] > 0:
                                    eng.wait_ge(TICK, before[i])
                                fn(eng).then_inc(TICK, self.ops[i][2])
                            if ename == "gpsimd":
                                eng.wait_ge(TICK, self.total)
                        return section

                    getattr(block, ename)(mk(myops, ename))


def ln_tile(p, out_ap, in_ap, D, scr, st, cst):
    rs, mean, rss, m2, var, rstd = (st[:, i:i + 1] for i in range(6))
    p.op("vector", lambda e: e.reduce_sum(rs, in_ap, axis=AX.X))
    p.op("vector", lambda e: e.tensor_scalar_mul(mean, rs, 1.0 / D))
    p.op("scalar", lambda e: e.activation(scr[:, 0:D], in_ap, AF.Square))
    p.op("vector", lambda e: e.reduce_sum(rss, scr[:, 0:D], axis=AX.X))
    p.op("vector", lambda e: e.tensor_mul(m2, mean, mean))
    p.op("vector", lambda e: e.tensor_scalar_mul(var, rss, 1.0 / D))
    p.op("vector", lambda e: e.tensor_sub(var, var, m2))
    p.op("scalar", lambda e: e.activation(rstd, var, AF.Ln, bias=cst[:, 0:1]))
    p.op("scalar", lambda e: e.activation(rstd, rstd, AF.Exp, scale=-0.5))
    p.op("vector", lambda e: e.tensor_scalar(out_ap, in_ap, mean, rstd, ALU.subtract, ALU.mult))


def tr(p, dst_ap, src_ap, ident_ap, pt):
    p.op("tensor", lambda e: e.transpose(pt, src_ap, ident_ap))
    p.op("scalar", lambda e: e.copy(dst_ap, pt))


def build_L1():
    nc = bass.Bass("TRN2", target_bir_lowering=False)
    x = nc.dram_tensor("x", [1024, 4096], F32, kind="ExternalInput").ap()
    wq = nc.dram_tensor("wq", [4096, 4096], F32, kind="ExternalInput").ap()
    wk = nc.dram_tensor("wk", [4096, 1024], F32, kind="ExternalInput").ap()
    wv = nc.dram_tensor("wv", [4096, 1024], F32, kind="ExternalInput").ap()
    identb_d = nc.dram_tensor("identb", [128, 128], BF16, kind="ExternalInput").ap()
    qT = nc.dram_tensor("qT", [1024, 1024], F32, kind="ExternalOutput").ap()
    kT = nc.dram_tensor("kT", [1024, 1024], F32, kind="ExternalOutput").ap()
    v = nc.dram_tensor("v", [1024, 1024], F32, kind="ExternalOutput").ap()

    with ExitStack() as es:
        def sb(name, shape, dt=F32):
            return es.enter_context(nc.sbuf_tensor("s_" + name, shape, dt))

        identb = sb("identb", [128, 128], BF16)
        xs = sb("xs", [128, 4096])
        scr = sb("scr", [128, 4096])
        st = sb("st", [128, 8])
        hb = sb("hb", [128, 4096], BF16)
        hT = sb("hT", [128, 32 * 1024], BF16)
        wtile = sb("wtile", [128, 512])
        t01 = sb("t01", [128, 128])
        t23 = sb("t23", [128, 128])
        wb = sb("wb", [128, 512], BF16)
        outs = sb("outs", [128, 1024])

        cst = sb("cst", [128, 2])
        p = Prog(nc)
        p.op("vector", lambda e: e.memset(cst[:, 0:1], EPS))
        p.op("vector", lambda e: e.memset(cst[:, 1:2], 1.0))
        p.dma(identb[:], identb_d)
        with nc.psum_tensor("pt", [128, 128], BF16) as pt:
            for t in range(8):
                p.dma(xs[:], x[t * 128:(t + 1) * 128, :])
                ln_tile(p, hb[:], xs[:], 4096, scr, st, cst)
                for i in range(32):
                    tr(p, hT[:, i * 1024 + t * 128: i * 1024 + t * 128 + 128],
                       hb[:, i * 128:(i + 1) * 128], identb[:], pt[:])
        with nc.psum_tensor("pacc", [128, 1024], F32) as pacc:
            for j in range(8):
                for i in range(32):
                    p.dma(wtile[:], wq[i * 128:(i + 1) * 128, j * 512:(j + 1) * 512])
                    p.op("vector", lambda e: e.tensor_add(t01[:], wtile[:, 0:128], wtile[:, 128:256]))
                    p.op("vector", lambda e: e.tensor_add(t23[:], wtile[:, 256:384], wtile[:, 384:512]))
                    p.op("vector", lambda e: e.tensor_add(wb[:, 0:128], t01[:], t23[:]))
                    for nk in range(2):
                        p.op("tensor", lambda e, i=i, nk=nk:
                             e.matmul(pacc[:, nk * 512:(nk + 1) * 512], wb[:, 0:128],
                                      hT[:, i * 1024 + nk * 512: i * 1024 + (nk + 1) * 512],
                                      start=(i == 0), stop=(i == 31)))
                p.op("scalar", lambda e: e.copy(outs[:], pacc[:]))
                p.dma(qT[j * 128:(j + 1) * 128, :], outs[:])
            for j in range(8):
                for i in range(32):
                    p.dma(wtile[:, 0:128], wk[i * 128:(i + 1) * 128, j * 128:(j + 1) * 128])
                    p.op("vector", lambda e: e.tensor_copy(wb[:, 0:128], wtile[:, 0:128]))
                    for nk in range(2):
                        p.op("tensor", lambda e, i=i, nk=nk:
                             e.matmul(pacc[:, nk * 512:(nk + 1) * 512], wb[:, 0:128],
                                      hT[:, i * 1024 + nk * 512: i * 1024 + (nk + 1) * 512],
                                      start=(i == 0), stop=(i == 31)))
                p.op("scalar", lambda e: e.copy(outs[:], pacc[:]))
                p.dma(kT[j * 128:(j + 1) * 128, :], outs[:])
        with ExitStack() as es2:
            pv = [es2.enter_context(nc.psum_tensor(f"pv{t}", [128, 512], F32)) for t in range(8)]
            for jc in range(2):
                for i in range(32):
                    p.dma(wtile[:], wv[i * 128:(i + 1) * 128, jc * 512:(jc + 1) * 512])
                    p.op("vector", lambda e: e.tensor_copy(wb[:], wtile[:]))
                    for t in range(8):
                        p.op("tensor", lambda e, i=i, t=t:
                             e.matmul(pv[t][:], hT[:, i * 1024 + t * 128: i * 1024 + t * 128 + 128],
                                      wb[:], start=(i == 0), stop=(i == 31)))
                for t in range(8):
                    p.op("scalar", lambda e, t=t: e.copy(outs[:, 0:512], pv[t][:]))
                    p.dma(v[t * 128:(t + 1) * 128, jc * 512:(jc + 1) * 512], outs[:, 0:512])
        p.build()
    return nc


def build_L2a():
    nc = bass.Bass("TRN2", target_bir_lowering=False)
    x = nc.dram_tensor("x", [1024, 4096], F32, kind="ExternalInput").ap()
    qT = nc.dram_tensor("qT", [1024, 1024], F32, kind="ExternalInput").ap()
    kTb = nc.dram_tensor("kTb", [1024, 4096], F32, kind="ExternalInput").ap()
    vb_in = nc.dram_tensor("vb", [4096, 1024], F32, kind="ExternalInput").ap()
    wo = nc.dram_tensor("wo", [1024, 4096], F32, kind="ExternalInput").ap()
    WT = nc.dram_tensor("WT", [4096, 4096], F32, kind="ExternalInput").ap()
    bias_d = nc.dram_tensor("bias", [1, 4096], F32, kind="ExternalInput").ap()
    AcatT = nc.dram_tensor("AcatT", [4096, 256], F32, kind="ExternalInput").ap()
    Bcat2 = nc.dram_tensor("Bcat2", [256, 4096], F32, kind="ExternalInput").ap()
    cf_d = nc.dram_tensor("cf", [16, 1], F32, kind="ExternalInput").ap()
    sc_d = nc.dram_tensor("sc", [16, 1], F32, kind="ExternalInput").ap()
    E_d = nc.dram_tensor("E", [16, 256], F32, kind="ExternalInput").ap()
    identb_d = nc.dram_tensor("identb", [128, 128], BF16, kind="ExternalInput").ap()
    x2 = nc.dram_tensor("x2", [1024, 4096], F32, kind="ExternalOutput").ap()
    x1s = nc.dram_tensor("x1s", [1024, 4096], F32, kind="Internal").ap()
    o0s = nc.dram_tensor("o0s", [1024, 4096], F32, kind="Internal").ap()

    with ExitStack() as es:
        def sb(name, shape, dt=F32):
            return es.enter_context(nc.sbuf_tensor("s_" + name, shape, dt))

        identb = sb("identb", [128, 128], BF16)
        oTb = sb("oTb", [128, 8 * 1024], BF16)
        st = sb("st", [128, 8])
        stage = sb("stage", [128, 4096])
        scr = sb("scr", [128, 4096])
        cst = sb("cst", [128, 2])
        p = Prog(nc)
        p.op("vector", lambda e: e.memset(cst[:, 0:1], EPS))
        p.op("vector", lambda e: e.memset(cst[:, 1:2], 1.0))
        p.dma(identb[:], identb_d)
        with ExitStack() as es2:
            def sb2(name, shape, dt=F32):
                return es2.enter_context(nc.sbuf_tensor("s_" + name, shape, dt))
            kb = sb2("kb", [128, 4096], BF16)
            vbb = sb2("vbb", [128, 4096], BF16)
            qb = sb2("qb", [128, 1024], BF16)
            S = sb2("S", [128, 4096])
            Pb = sb2("Pb", [128, 4096], BF16)
            PT = sb2("PT", [128, 4096], BF16)
            psS = es2.enter_context(nc.psum_tensor("psS", [128, 512], F32))
            ptb = es2.enter_context(nc.psum_tensor("ptb", [128, 128], BF16))
            psO = es2.enter_context(nc.psum_tensor("psO", [128, 128], F32))
            vre = vb_in.rearrange("(n p) d -> p n d", p=128)
            stg3 = stage.rearrange("p (n d) -> p n d", d=128)
            for h in range(8):
                p.dma(stage[:], kTb[h * 128:(h + 1) * 128, :])
                p.op("vector", lambda e: e.tensor_copy(kb[:], stage[:]))
                for n0 in range(0, 32, 8):
                    p.dma(stg3[:, n0:n0 + 8, :], vre[:, n0:n0 + 8, h * 128:(h + 1) * 128])
                p.op("vector", lambda e: e.tensor_copy(vbb[:], stage[:]))
                p.dma(stage[:, 0:1024], qT[h * 128:(h + 1) * 128, :])
                p.op("vector", lambda e: e.tensor_scalar_mul(qb[:], stage[:, 0:1024], 1.0 / math.sqrt(128.0)))
                for t in range(8):
                    for kc in range(8):
                        p.op("tensor", lambda e, t=t, kc=kc:
                             e.matmul(psS[:], qb[:, t * 128:(t + 1) * 128],
                                      kb[:, kc * 512:(kc + 1) * 512], start=True, stop=True))
                        p.op("scalar", lambda e, kc=kc: e.copy(S[:, kc * 512:(kc + 1) * 512], psS[:]))
                    p.op("vector", lambda e: e.reduce_max(st[:, 0:1], S[:], axis=AX.X))
                    p.op("vector", lambda e: e.tensor_scalar_mul(st[:, 1:2], st[:, 0:1], -1.0))
                    p.op("scalar", lambda e: e.activation(stage[:], S[:], AF.Exp, bias=st[:, 1:2]))
                    p.op("vector", lambda e: e.reduce_sum(st[:, 2:3], stage[:], axis=AX.X))
                    p.op("vector", lambda e: e.reciprocal(st[:, 3:4], st[:, 2:3]))
                    p.op("vector", lambda e: e.tensor_scalar_mul(Pb[:], stage[:], st[:, 3:4]))
                    for kc in range(32):
                        tr(p, PT[:, kc * 128:(kc + 1) * 128], Pb[:, kc * 128:(kc + 1) * 128],
                           identb[:], ptb[:])
                    for kc in range(32):
                        p.op("tensor", lambda e, kc=kc:
                             e.matmul(psO[:], vbb[:, kc * 128:(kc + 1) * 128],
                                      PT[:, kc * 128:(kc + 1) * 128], start=(kc == 0), stop=(kc == 31)))
                    p.op("scalar", lambda e, h=h, t=t:
                         e.copy(oTb[:, h * 1024 + t * 128: h * 1024 + t * 128 + 128], psO[:]))
        h2T = sb("h2T", [128, 32 * 1024], BF16)
        with ExitStack() as es3:
            def sb3(name, shape, dt=F32):
                return es3.enter_context(nc.sbuf_tensor("s_" + name, shape, dt))
            wtile = sb3("wtile", [128, 512])
            wbb = sb3("wbb", [128, 512], BF16)
            x1t = sb3("x1t", [128, 4096])
            h2b = sb3("h2b", [128, 4096], BF16)
            psW = es3.enter_context(nc.psum_tensor("psW", [128, 512], F32))
            ptb2 = es3.enter_context(nc.psum_tensor("ptb2", [128, 128], BF16))
            for t in range(8):
                for c in range(8):
                    for h in range(8):
                        p.dma(wtile[:], wo[h * 128:(h + 1) * 128, c * 512:(c + 1) * 512])
                        p.op("vector", lambda e: e.tensor_copy(wbb[:], wtile[:]))
                        p.op("tensor", lambda e, t=t, h=h:
                             e.matmul(psW[:], oTb[:, h * 1024 + t * 128: h * 1024 + t * 128 + 128],
                                      wbb[:], start=(h == 0), stop=(h == 7)))
                    p.op("scalar", lambda e, c=c: e.copy(stage[:, c * 512:(c + 1) * 512], psW[:]))
                ln_tile(p, stage[:], stage[:], 4096, scr, st, cst)
                p.dma(x1t[:], x[t * 128:(t + 1) * 128, :])
                p.op("vector", lambda e: e.tensor_add(x1t[:], x1t[:], stage[:]))
                p.dma(x1s[t * 128:(t + 1) * 128, :], x1t[:])
                ln_tile(p, h2b[:], x1t[:], 4096, scr, st, cst)
                for i in range(32):
                    tr(p, h2T[:, i * 1024 + t * 128: i * 1024 + t * 128 + 128],
                       h2b[:, i * 128:(i + 1) * 128], identb[:], ptb2[:])
        with ExitStack() as es4:
            def sb4(name, shape, dt=F32):
                return es4.enter_context(nc.sbuf_tensor("s_" + name, shape, dt))
            w16 = sb4("w16", [16, 4])
            Et = sb4("Et", [16, 256])
            wexp = sb4("wexp", [128, 2])
            B2b = sb4("B2b", [128, 2 * 4096], BF16)
            Acatb = sb4("Acatb", [128, 32 * 256], BF16)
            xaT = sb4("xaT", [128, 8 * 256], BF16)
            xab = sb4("xab", [128, 256], BF16)
            with ExitStack() as es5:
                psE = es5.enter_context(nc.psum_tensor("psE", [128, 1], F32))
                psA = es5.enter_context(nc.psum_tensor("psA", [128, 256], F32))
                ptb3 = es5.enter_context(nc.psum_tensor("ptb3", [128, 128], BF16))
                p.dma(w16[:, 0:1], cf_d)
                p.dma(w16[:, 1:2], sc_d)
                p.op("vector", lambda e: e.tensor_mul(w16[:, 2:3], w16[:, 0:1], w16[:, 1:2]))
                p.dma(Et[:], E_d)
                for half in range(2):
                    p.op("tensor", lambda e, half=half:
                         e.matmul(psE[:], Et[:, half * 128:(half + 1) * 128], w16[:, 2:3],
                                  start=True, stop=True))
                    p.op("scalar", lambda e, half=half: e.copy(wexp[:, half:half + 1], psE[:]))
                for er in range(2):
                    p.dma(stage[:], Bcat2[er * 128:(er + 1) * 128, :])
                    p.op("vector", lambda e, er=er:
                         e.tensor_scalar_mul(B2b[:, er * 4096:(er + 1) * 4096], stage[:], wexp[:, er:er + 1]))
                for i in range(32):
                    p.dma(stage[:, 0:256], AcatT[i * 128:(i + 1) * 128, :])
                    p.op("vector", lambda e, i=i: e.tensor_copy(Acatb[:, i * 256:(i + 1) * 256], stage[:, 0:256]))
                for t in range(8):
                    for i in range(32):
                        p.op("tensor", lambda e, i=i, t=t:
                             e.matmul(psA[:], h2T[:, i * 1024 + t * 128: i * 1024 + t * 128 + 128],
                                      Acatb[:, i * 256:(i + 1) * 256], start=(i == 0), stop=(i == 31)))
                    p.op("vector", lambda e: e.tensor_copy(xab[:], psA[:]))
                    for er in range(2):
                        tr(p, xaT[:, t * 256 + er * 128: t * 256 + (er + 1) * 128],
                           xab[:, er * 128:(er + 1) * 128], identb[:], ptb3[:])
            wtile2 = sb4("wtile2", [128, 512])
            wbb2 = sb4("wbb2", [128, 512], BF16)
            with ExitStack() as es6:
                pW8 = [es6.enter_context(nc.psum_tensor(f"pW{t}", [128, 512], F32)) for t in range(8)]
                for c in range(8):
                    for i in range(32):
                        p.dma(wtile2[:], WT[i * 128:(i + 1) * 128, c * 512:(c + 1) * 512])
                        p.op("vector", lambda e: e.tensor_copy(wbb2[:], wtile2[:]))
                        for t in range(8):
                            p.op("tensor", lambda e, i=i, t=t:
                                 e.matmul(pW8[t][:], h2T[:, i * 1024 + t * 128: i * 1024 + t * 128 + 128],
                                          wbb2[:], start=(i == 0), stop=False))
                    for t in range(8):
                        for er in range(2):
                            p.op("tensor", lambda e, t=t, er=er, c=c:
                                 e.matmul(pW8[t][:], xaT[:, t * 256 + er * 128: t * 256 + (er + 1) * 128],
                                          B2b[:, er * 4096 + c * 512: er * 4096 + (c + 1) * 512],
                                          start=False, stop=(er == 1)))
                        p.op("scalar", lambda e, t=t: e.copy(stage[:, 0:512], pW8[t][:]))
                        p.dma(o0s[t * 128:(t + 1) * 128, c * 512:(c + 1) * 512], stage[:, 0:512])
        with ExitStack() as es7:
            def sb7(name, shape, dt=F32):
                return es7.enter_context(nc.sbuf_tensor("s_" + name, shape, dt))
            bias_sb = sb7("bias_sb", [1, 4096])
            bias_bc = sb7("bias_bc", [128, 4096])
            ones1 = sb7("ones1", [1, 128])
            x1t2 = sb7("x1t2", [128, 4096])
            psB = es7.enter_context(nc.psum_tensor("psB", [128, 512], F32))
            p.dma(bias_sb[:], bias_d)
            p.op("vector", lambda e: e.memset(ones1[:], 1.0))
            for c in range(8):
                p.op("tensor", lambda e, c=c:
                     e.matmul(psB[:], ones1[:], bias_sb[0:1, c * 512:(c + 1) * 512], start=True, stop=True))
                p.op("scalar", lambda e, c=c: e.copy(bias_bc[:, c * 512:(c + 1) * 512], psB[:]))
            for t in range(8):
                p.dma(stage[:], o0s[t * 128:(t + 1) * 128, :])
                p.op("vector", lambda e: e.tensor_add(stage[:], stage[:], bias_bc[:]))
                ln_tile(p, stage[:], stage[:], 4096, scr, st, cst)
                p.dma(x1t2[:], x1s[t * 128:(t + 1) * 128, :])
                p.op("vector", lambda e: e.tensor_add(x1t2[:], x1t2[:], stage[:]))
                p.dma(x2[t * 128:(t + 1) * 128, :], x1t2[:])
        p.build()
    return nc


def build_L2b():
    nc = bass.Bass("TRN2", target_bir_lowering=False)
    x2p = nc.dram_tensor("x2p", [1152, 4096], F32, kind="ExternalInput").ap()
    inT = nc.dram_tensor("inT", [4096, 18560], F32, kind="ExternalInput").ap()
    convw = nc.dram_tensor("convw", [10240, 4], F32, kind="ExternalInput").ap()
    convb = nc.dram_tensor("convb", [10240, 1], F32, kind="ExternalInput").ap()
    identb_d = nc.dram_tensor("identb", [128, 128], BF16, kind="ExternalInput").ap()
    xbcT = nc.dram_tensor("xbcT", [10240, 1024], F32, kind="ExternalOutput").ap()
    z = nc.dram_tensor("z", [1024, 8192], F32, kind="ExternalOutput").ap()
    dtr = nc.dram_tensor("dtr", [1024, 128], F32, kind="ExternalOutput").ap()

    with ExitStack() as es:
        def sb(name, shape, dt=F32):
            return es.enter_context(nc.sbuf_tensor("s_" + name, shape, dt))

        identb = sb("identb", [128, 128], BF16)
        xs = sb("xs", [128, 4096])
        scr = sb("scr", [128, 4096])
        st = sb("st", [128, 8])
        hmb = sb("hmb", [128, 4096], BF16)
        hmT = sb("hmT", [128, 32 * 1152], BF16)
        wstage = sb("wstage", [128, 512])
        wb = sb("wb", [128, 512], BF16)
        outsb = sb("outsb", [128, 512])

        cst = sb("cst", [128, 2])
        p = Prog(nc)
        p.op("vector", lambda e: e.memset(cst[:, 0:1], EPS))
        p.op("vector", lambda e: e.memset(cst[:, 1:2], 1.0))
        p.dma(identb[:], identb_d)
        with nc.psum_tensor("pt", [128, 128], BF16) as pt:
            for t in range(9):
                p.dma(xs[:], x2p[t * 128:(t + 1) * 128, :])
                ln_tile(p, hmb[:], xs[:], 4096, scr, st, cst)
                for i in range(32):
                    tr(p, hmT[:, i * 1152 + t * 128: i * 1152 + t * 128 + 128],
                       hmb[:, i * 128:(i + 1) * 128], identb[:], pt[:])

        def zwrite(t, dram, c0, cw):
            if t == 0:
                p.dma(dram[0:125, c0:c0 + cw], outsb[3:128, 0:cw])
            else:
                p.dma(dram[t * 128 - 3: t * 128 + 125, c0:c0 + cw], outsb[:, 0:cw])

        with ExitStack() as es2:
            pz = [es2.enter_context(nc.psum_tensor(f"pz{t}", [128, 512], F32)) for t in range(8)]
            for c in range(16):
                for i in range(32):
                    p.dma(wstage[:], inT[i * 128:(i + 1) * 128, c * 512:(c + 1) * 512])
                    p.op("vector", lambda e: e.tensor_copy(wb[:], wstage[:]))
                    for t in range(8):
                        p.op("tensor", lambda e, i=i, t=t:
                             e.matmul(pz[t][:], hmT[:, i * 1152 + t * 128: i * 1152 + t * 128 + 128],
                                      wb[:], start=(i == 0), stop=(i == 31)))
                for t in range(8):
                    p.op("scalar", lambda e, t=t: e.copy(outsb[:], pz[t][:]))
                    zwrite(t, z, c * 512, 512)
            for c in range(16):
                for i in range(32):
                    p.dma(wstage[:], inT[i * 128:(i + 1) * 128, c * 512:(c + 1) * 512])
                    p.op("vector", lambda e: e.tensor_copy(wb[:], wstage[:]))
                    p.op("tensor", lambda e, i=i:
                         e.matmul(pz[0][:], hmT[:, i * 1152 + 1024: i * 1152 + 1152], wb[:],
                                  start=(i == 0), stop=(i == 31)))
                p.op("scalar", lambda e: e.copy(outsb[:], pz[0][:]))
                p.dma(z[1021:1024, c * 512:(c + 1) * 512], outsb[0:3, :])
            for i in range(32):
                p.dma(wstage[:, 0:128], inT[i * 128:(i + 1) * 128, 18432:18560])
                p.op("vector", lambda e: e.tensor_copy(wb[:, 0:128], wstage[:, 0:128]))
                for t in range(8):
                    p.op("tensor", lambda e, i=i, t=t:
                         e.matmul(pz[t][:, 0:128], hmT[:, i * 1152 + t * 128: i * 1152 + t * 128 + 128],
                                  wb[:, 0:128], start=(i == 0), stop=(i == 31)))
            for t in range(8):
                p.op("scalar", lambda e, t=t: e.copy(outsb[:, 0:128], pz[t][:, 0:128]))
                zwrite(t, dtr, 0, 128)
            for i in range(32):
                p.dma(wstage[:, 0:128], inT[i * 128:(i + 1) * 128, 18432:18560])
                p.op("vector", lambda e: e.tensor_copy(wb[:, 0:128], wstage[:, 0:128]))
                p.op("tensor", lambda e, i=i:
                     e.matmul(pz[0][:, 0:128], hmT[:, i * 1152 + 1024: i * 1152 + 1152],
                              wb[:, 0:128], start=(i == 0), stop=(i == 31)))
            p.op("scalar", lambda e: e.copy(outsb[:, 0:128], pz[0][:, 0:128]))
            p.dma(dtr[1021:1024, :], outsb[0:3, 0:128])
        with ExitStack() as es3:
            px = es3.enter_context(nc.psum_tensor("px", [128, 1152], F32))
            xbcj = es3.enter_context(nc.sbuf_tensor("xbcj", [128, 1152], F32))
            acc1 = es3.enter_context(nc.sbuf_tensor("acc1", [128, 1024], F32))
            acc2 = es3.enter_context(nc.sbuf_tensor("acc2", [128, 1024], F32))
            cw = es3.enter_context(nc.sbuf_tensor("cw", [128, 4], F32))
            cb = es3.enter_context(nc.sbuf_tensor("cb", [128, 1], F32))
            for j in range(80):
                p.dma(cw[:], convw[j * 128:(j + 1) * 128, :])
                p.dma(cb[:], convb[j * 128:(j + 1) * 128, :])
                for i in range(32):
                    p.dma(wstage[:, 0:128], inT[i * 128:(i + 1) * 128, 8192 + j * 128: 8192 + (j + 1) * 128])
                    p.op("vector", lambda e: e.tensor_copy(wb[:, 0:128], wstage[:, 0:128]))
                    for (o0, nw_) in ((0, 512), (512, 512), (1024, 128)):
                        p.op("tensor", lambda e, i=i, o0=o0, nw_=nw_:
                             e.matmul(px[:, o0:o0 + nw_], wb[:, 0:128],
                                      hmT[:, i * 1152 + o0: i * 1152 + o0 + nw_],
                                      start=(i == 0), stop=(i == 31)))
                p.op("scalar", lambda e: e.copy(xbcj[:], px[:]))
                p.op("vector", lambda e: e.tensor_scalar_mul(acc1[:], xbcj[:, 0:1024], cw[:, 0:1]))
                p.op("vector", lambda e: e.scalar_tensor_tensor(acc2[:], xbcj[:, 1:1025], cw[:, 1:2], acc1[:], ALU.mult, ALU.add))
                p.op("vector", lambda e: e.scalar_tensor_tensor(acc1[:], xbcj[:, 2:1026], cw[:, 2:3], acc2[:], ALU.mult, ALU.add))
                p.op("vector", lambda e: e.scalar_tensor_tensor(acc2[:], xbcj[:, 3:1027], cw[:, 3:4], acc1[:], ALU.mult, ALU.add))
                p.op("scalar", lambda e: e.activation(xbcj[:, 0:1024], acc2[:], AF.Silu, bias=cb[:, 0:1]))
                p.dma(xbcT[j * 128:(j + 1) * 128, :], xbcj[:, 0:1024])
        p.build()
    return nc


def build_L3():
    nc = bass.Bass("TRN2", target_bir_lowering=False)
    xT = nc.dram_tensor("xT", [2048, 4096], F32, kind="ExternalInput").ap()
    BT = nc.dram_tensor("BT", [256, 4096], F32, kind="ExternalInput").ap()
    CT = nc.dram_tensor("CT", [256, 4096], F32, kind="ExternalInput").ap()
    z_d = nc.dram_tensor("z", [4096, 2048], F32, kind="ExternalInput").ap()
    dtr = nc.dram_tensor("dtr", [4096, 32], F32, kind="ExternalInput").ap()
    dtb_d = nc.dram_tensor("dtb", [1, 32], F32, kind="ExternalInput").ap()
    alog_d = nc.dram_tensor("alog", [1, 32], F32, kind="ExternalInput").ap()
    Dh_d = nc.dram_tensor("Dh", [1, 32], F32, kind="ExternalInput").ap()
    cumT_d = nc.dram_tensor("cumT", [128, 128], F32, kind="ExternalInput").ap()
    maskA_d = nc.dram_tensor("maskA", [128, 256], F32, kind="ExternalInput").ap()
    identf_d = nc.dram_tensor("identf", [128, 128], F32, kind="ExternalInput").ap()
    identb_d = nc.dram_tensor("identb", [128, 128], BF16, kind="ExternalInput").ap()
    y2 = nc.dram_tensor("y2", [4096, 2048], F32, kind="ExternalOutput").ap()

    with ExitStack() as es:
        def sb(name, shape, dt=F32):
            return es.enter_context(nc.sbuf_tensor("s_" + name, shape, dt))
        identf = sb("identf", [128, 128])
        identb = sb("identb", [128, 128], BF16)
        cumT = sb("cumT", [128, 128])
        maskA = sb("maskA", [128, 256])
        onesT = sb("onesT", [128, 128])
        ones1 = sb("ones1", [1, 128])
        St = sb("St", [128, 32 * 64])
        dtb_bc = sb("dtb_bc", [128, 32])
        Abc = sb("Abc", [128, 32])
        Dbc = sb("Dbc", [128, 32])
        dt0 = sb("dt0", [128, 32])
        dt1 = sb("dt1", [128, 32])
        dA0 = sb("dA0", [128, 32])
        dA1 = sb("dA1", [128, 32])
        Acs0 = sb("Acs0", [128, 32])
        Acs1 = sb("Acs1", [128, 32])
        nAcs0 = sb("nAcs0", [128, 32])
        nAcs1 = sb("nAcs1", [128, 32])
        lastrow = sb("lastrow", [1, 32])
        Albc = sb("Albc", [128, 32])
        Dec0 = sb("Dec0", [128, 32])
        Dec1 = sb("Dec1", [128, 32])
        cdec = sb("cdec", [128, 32])
        AcsT = sb("AcsT", [32, 256])
        tmp32 = sb("tmp32", [128, 32])
        crow = sb("crow", [1, 256])
        BTb = [sb(f"BTb{g}", [128, 256], BF16) for g in range(2)]
        CTb = [sb(f"CTb{g}", [128, 256], BF16) for g in range(2)]
        CTf = [sb(f"CTf{g}", [128, 256]) for g in range(2)]
        S1Ta = [sb(f"S1Ta{g}", [128, 256]) for g in range(2)]
        S1Tb = [sb(f"S1Tb{g}", [128, 128]) for g in range(2)]
        Btm = [sb(f"Btm{g}", [128, 256], BF16) for g in range(2)]
        stg = sb("stg", [128, 256])
        Ea = sb("Ea", [128, 256])
        Eb = sb("Eb", [128, 128])
        S2a = sb("S2a", [128, 256], BF16)
        S2b = sb("S2b", [128, 128], BF16)
        xtb = sb("xtb", [64, 256], BF16)
        x_hf = [sb(f"x_hf{s}", [128, 64]) for s in range(2)]
        xdt = [sb(f"xdt{s}", [128, 64], BF16) for s in range(2)]
        xdd = [sb(f"xdd{s}", [128, 64], BF16) for s in range(2)]
        odE = sb("odE", [128, 256])
        Cod = sb("Cod", [128, 256], BF16)
        StRb = sb("StRb", [128, 64], BF16)
        y0 = sb("y0", [128, 64])
        zst = sb("zst", [128, 64])
        zsil = sb("zsil", [128, 64])

        pcs = es.enter_context(nc.psum_tensor("pcs", [128, 32], F32))
        ptf = es.enter_context(nc.psum_tensor("ptf", [32, 128], F32))
        pS1 = es.enter_context(nc.psum_tensor("pS1", [128, 256], F32))
        ptb = es.enter_context(nc.psum_tensor("ptb", [128, 128], BF16))
        pDa = es.enter_context(nc.psum_tensor("pDa", [128, 256], F32))
        pDb = es.enter_context(nc.psum_tensor("pDb", [128, 128], F32))
        pY = es.enter_context(nc.psum_tensor("pY", [128, 64], F32))
        pSt = es.enter_context(nc.psum_tensor("pSt", [128, 64], F32))

        cst = sb("cst", [128, 2])
        p = Prog(nc)
        p.op("vector", lambda e: e.memset(cst[:, 0:1], EPS))
        p.op("vector", lambda e: e.memset(cst[:, 1:2], 1.0))
        p.dma(identf[:], identf_d)
        p.dma(identb[:], identb_d)
        p.dma(cumT[:], cumT_d)
        p.dma(maskA[:], maskA_d)
        p.op("vector", lambda e: e.memset(onesT[:], 1.0))
        p.op("vector", lambda e: e.memset(ones1[:], 1.0))
        p.op("vector", lambda e: e.memset(St[:], 0.0))
        p.dma(lastrow[:], dtb_d)
        p.op("tensor", lambda e: e.matmul(pcs[:], ones1[:], lastrow[:], start=True, stop=True))
        p.op("scalar", lambda e: e.copy(dtb_bc[:], pcs[:]))
        p.dma(lastrow[:], alog_d)
        p.op("tensor", lambda e: e.matmul(pcs[:], ones1[:], lastrow[:], start=True, stop=True))
        p.op("scalar", lambda e: e.copy(Abc[:], pcs[:]))
        p.op("scalar", lambda e: e.activation(Abc[:], Abc[:], AF.Exp))
        p.op("vector", lambda e: e.tensor_scalar_mul(Abc[:], Abc[:], -1.0))
        p.dma(lastrow[:], Dh_d)
        p.op("tensor", lambda e: e.matmul(pcs[:], ones1[:], lastrow[:], start=True, stop=True))
        p.op("scalar", lambda e: e.copy(Dbc[:], pcs[:]))

        for c in range(16):
            c0 = c * 256
            for stl, dtX in ((0, dt0), (1, dt1)):
                p.dma(tmp32[:], dtr[c0 + stl * 128: c0 + (stl + 1) * 128, :])
                p.op("vector", lambda e, dtX=dtX: e.tensor_add(dtX[:], tmp32[:], dtb_bc[:]))
                p.op("scalar", lambda e, dtX=dtX: e.activation(tmp32[:], dtX[:], AF.Exp))
                p.op("scalar", lambda e, dtX=dtX: e.activation(dtX[:], tmp32[:], AF.Ln, bias=cst[:, 1:2]))
            p.op("vector", lambda e: e.tensor_mul(dA0[:], dt0[:], Abc[:]))
            p.op("vector", lambda e: e.tensor_mul(dA1[:], dt1[:], Abc[:]))
            p.op("tensor", lambda e: e.matmul(pcs[:], cumT[:], dA0[:], start=True, stop=True))
            p.op("scalar", lambda e: e.copy(Acs0[:], pcs[:]))
            p.op("tensor", lambda e: e.matmul(pcs[:], onesT[:], dA0[:], start=True, stop=False))
            p.op("tensor", lambda e: e.matmul(pcs[:], cumT[:], dA1[:], start=False, stop=True))
            p.op("scalar", lambda e: e.copy(Acs1[:], pcs[:]))
            p.op("vector", lambda e: e.tensor_scalar_mul(nAcs0[:], Acs0[:], -1.0))
            p.op("vector", lambda e: e.tensor_scalar_mul(nAcs1[:], Acs1[:], -1.0))
            p.dma(lastrow[:], Acs1[127:128, :])
            p.op("tensor", lambda e: e.matmul(pcs[:], ones1[:], lastrow[:], start=True, stop=True))
            p.op("scalar", lambda e: e.copy(Albc[:], pcs[:]))
            p.op("vector", lambda e: e.tensor_sub(tmp32[:], Albc[:], Acs0[:]))
            p.op("scalar", lambda e: e.activation(Dec0[:], tmp32[:], AF.Exp))
            p.op("vector", lambda e: e.tensor_sub(tmp32[:], Albc[:], Acs1[:]))
            p.op("scalar", lambda e: e.activation(Dec1[:], tmp32[:], AF.Exp))
            p.op("scalar", lambda e: e.activation(cdec[:], Albc[:], AF.Exp))
            for stl, acs in ((0, Acs0), (1, Acs1)):
                p.op("tensor", lambda e, acs=acs: e.transpose(ptf[:], acs[:], identf[:]))
                p.op("scalar", lambda e, stl=stl: e.copy(AcsT[:, stl * 128:(stl + 1) * 128], ptf[:]))
            for g in range(2):
                p.dma(stg[:], BT[g * 128:(g + 1) * 128, c0:c0 + 256])
                p.op("vector", lambda e, g=g: e.tensor_copy(BTb[g][:], stg[:]))
                p.dma(stg[:], CT[g * 128:(g + 1) * 128, c0:c0 + 256])
                p.op("vector", lambda e, g=g: e.tensor_copy(CTf[g][:], stg[:]))
                p.op("vector", lambda e, g=g: e.tensor_copy(CTb[g][:], stg[:]))
                p.op("tensor", lambda e, g=g:
                     e.matmul(pS1[:], BTb[g][:, 0:128], CTb[g][:], start=True, stop=True))
                p.op("scalar", lambda e, g=g: e.copy(S1Ta[g][:], pS1[:]))
                p.op("tensor", lambda e, g=g:
                     e.matmul(pS1[:, 0:128], BTb[g][:, 128:256], CTb[g][:, 128:256], start=True, stop=True))
                p.op("scalar", lambda e, g=g: e.copy(S1Tb[g][:], pS1[:, 0:128]))
                for stl in range(2):
                    p.op("tensor", lambda e, g=g, stl=stl:
                         e.transpose(ptb[:], BTb[g][:, stl * 128:(stl + 1) * 128], identb[:]))
                    p.op("scalar", lambda e, g=g, stl=stl:
                         e.copy(Btm[g][:, stl * 128:(stl + 1) * 128], ptb[:]))
            for h in range(32):
                g = h // 16
                p.dma(crow[:], AcsT[h:h + 1, :])
                p.op("tensor", lambda e: e.matmul(pDa[:], ones1[:], crow[:], start=True, stop=False))
                p.op("tensor", lambda e: e.matmul(pDa[:], identf[:], maskA[:], start=False, stop=True))
                p.op("tensor", lambda e: e.matmul(pDb[:], ones1[:], crow[0:1, 128:256], start=True, stop=False))
                p.op("tensor", lambda e: e.matmul(pDb[:], identf[:], maskA[:, 0:128], start=False, stop=True))
                p.op("scalar", lambda e, h=h: e.activation(Ea[:], pDa[:], AF.Exp, bias=nAcs0[:, h:h + 1]))
                p.op("scalar", lambda e, h=h: e.activation(Eb[:], pDb[:], AF.Exp, bias=nAcs1[:, h:h + 1]))
                p.op("vector", lambda e, g=g: e.tensor_mul(S2a[:], Ea[:], S1Ta[g][:]))
                p.op("vector", lambda e, g=g: e.tensor_mul(S2b[:], Eb[:], S1Tb[g][:]))
                p.dma(stg[0:64, 0:256], xT[h * 64:(h + 1) * 64, c0:c0 + 256])
                p.op("vector", lambda e: e.tensor_copy(xtb[:], stg[0:64, 0:256]))
                for stl in range(2):
                    p.op("tensor", lambda e, stl=stl:
                         e.transpose(ptb[:, 0:64], xtb[:, stl * 128:(stl + 1) * 128], identb[0:64, 0:64]))
                    p.op("scalar", lambda e, stl=stl: e.copy(x_hf[stl][:], ptb[:, 0:64]))
                for stl, dtX, decX in ((0, dt0, Dec0), (1, dt1, Dec1)):
                    p.op("vector", lambda e, stl=stl, dtX=dtX, h=h:
                         e.tensor_scalar_mul(xdt[stl][:], x_hf[stl][:], dtX[:, h:h + 1]))
                    p.op("vector", lambda e, stl=stl, dtX=dtX, decX=decX, h=h:
                         e.tensor_scalar(xdd[stl][:], x_hf[stl][:], dtX[:, h:h + 1], decX[:, h:h + 1],
                                         ALU.mult, ALU.mult))
                p.op("tensor", lambda e: e.matmul(pDa[:], ones1[:], crow[:], start=True, stop=True))
                p.op("scalar", lambda e: e.activation(odE[:], pDa[:], AF.Exp))
                p.op("vector", lambda e, g=g: e.tensor_mul(Cod[:], CTf[g][:], odE[:]))
                p.op("vector", lambda e, h=h: e.tensor_copy(StRb[:], St[:, h * 64:(h + 1) * 64]))
                p.op("tensor", lambda e: e.matmul(pY[:], S2a[:, 0:128], xdt[0][:], start=True, stop=False))
                p.op("tensor", lambda e: e.matmul(pY[:], Cod[:, 0:128], StRb[:], start=False, stop=True))
                p.op("vector", lambda e, h=h:
                     e.scalar_tensor_tensor(y0[:], x_hf[0][:], Dbc[:, h:h + 1], pY[:], ALU.mult, ALU.add))
                p.dma(zst[:], z_d[c0:c0 + 128, h * 64:(h + 1) * 64])
                p.op("scalar", lambda e: e.activation(zsil[:], zst[:], AF.Silu))
                p.op("vector", lambda e: e.tensor_mul(y0[:], y0[:], zsil[:]))
                p.dma(y2[c0:c0 + 128, h * 64:(h + 1) * 64], y0[:])
                p.op("tensor", lambda e: e.matmul(pY[:], S2a[:, 128:256], xdt[0][:], start=True, stop=False))
                p.op("tensor", lambda e: e.matmul(pY[:], S2b[:], xdt[1][:], start=False, stop=False))
                p.op("tensor", lambda e: e.matmul(pY[:], Cod[:, 128:256], StRb[:], start=False, stop=True))
                p.op("vector", lambda e, h=h:
                     e.scalar_tensor_tensor(y0[:], x_hf[1][:], Dbc[:, h:h + 1], pY[:], ALU.mult, ALU.add))
                p.dma(zst[:], z_d[c0 + 128:c0 + 256, h * 64:(h + 1) * 64])
                p.op("scalar", lambda e: e.activation(zsil[:], zst[:], AF.Silu))
                p.op("vector", lambda e: e.tensor_mul(y0[:], y0[:], zsil[:]))
                p.dma(y2[c0 + 128:c0 + 256, h * 64:(h + 1) * 64], y0[:])
                p.op("tensor", lambda e, g=g: e.matmul(pSt[:], Btm[g][:, 0:128], xdd[0][:], start=True, stop=False))
                p.op("tensor", lambda e, g=g: e.matmul(pSt[:], Btm[g][:, 128:256], xdd[1][:], start=False, stop=True))
                p.op("vector", lambda e, h=h:
                     e.scalar_tensor_tensor(St[:, h * 64:(h + 1) * 64], St[:, h * 64:(h + 1) * 64],
                                            cdec[:, h:h + 1], pSt[:], ALU.mult, ALU.add))
        p.build()
    return nc


def build_L4():
    nc = bass.Bass("TRN2", target_bir_lowering=False)
    x2 = nc.dram_tensor("x2", [1024, 4096], F32, kind="ExternalInput").ap()
    y2 = nc.dram_tensor("y2", [1024, 8192], F32, kind="ExternalInput").ap()
    nw_d = nc.dram_tensor("nw", [1, 8192], F32, kind="ExternalInput").ap()
    opT = nc.dram_tensor("opT", [8192, 4096], F32, kind="ExternalInput").ap()
    Wqt = nc.dram_tensor("Wqt", [4096, 64], F32, kind="ExternalInput").ap()
    Wrt = nc.dram_tensor("Wrt", [64, 48], F32, kind="ExternalInput").ap()
    akT = nc.dram_tensor("akT", [64, 1024], F32, kind="ExternalInput").ap()
    lrel_d = nc.dram_tensor("lrel", [1, 1024], F32, kind="ExternalInput").ap()
    avT = nc.dram_tensor("avT", [4096, 1024], F32, kind="ExternalInput").ap()
    Wvdt = nc.dram_tensor("Wvdt", [4096, 256], F32, kind="ExternalInput").ap()
    Wvut = nc.dram_tensor("Wvut", [256, 4096], F32, kind="ExternalInput").ap()
    Wu1t = nc.dram_tensor("Wu1t", [4096, 1024], F32, kind="ExternalInput").ap()
    bu1_d = nc.dram_tensor("bu1", [1, 1024], F32, kind="ExternalInput").ap()
    Wu2_d = nc.dram_tensor("Wu2", [1, 1024], F32, kind="ExternalInput").ap()
    bu2_d = nc.dram_tensor("bu2", [1, 1], F32, kind="ExternalInput").ap()
    identb_d = nc.dram_tensor("identb", [128, 128], BF16, kind="ExternalInput").ap()
    x3 = nc.dram_tensor("x3", [1024, 4096], F32, kind="ExternalOutput").ap()
    aux = nc.dram_tensor("aux", [1024, 4096], F32, kind="ExternalOutput").ap()
    sl_o = nc.dram_tensor("sl", [1024, 1], F32, kind="ExternalOutput").ap()
    lv_o = nc.dram_tensor("lv", [1024, 1], F32, kind="ExternalOutput").ap()
    lvs_o = nc.dram_tensor("lvs", [1, 1], F32, kind="ExternalOutput").ap()
    o_scr = nc.dram_tensor("o_scr", [1024, 4096], F32, kind="Internal").ap()
    T2s = nc.dram_tensor("T2s", [1024, 4096], BF16, kind="Internal").ap()

    with ExitStack() as es:
        def sb(name, shape, dt=F32):
            return es.enter_context(nc.sbuf_tensor("s_" + name, shape, dt))
        identb = sb("identb", [128, 128], BF16)
        st = sb("st", [128, 8])
        scr = sb("scr", [128, 8192])
        stage = sb("stage", [128, 8192])
        wtile = sb("wtile", [128, 512])
        wbb = sb("wbb", [128, 512], BF16)
        ones1 = sb("ones1", [1, 128])
        onescol = sb("onescol", [128, 1])
        cst = sb("cst", [128, 2])
        p = Prog(nc)
        p.op("vector", lambda e: e.memset(cst[:, 0:1], EPS))
        p.op("vector", lambda e: e.memset(cst[:, 1:2], 1.0))
        p.dma(identb[:], identb_d)
        p.op("vector", lambda e: e.memset(ones1[:], 1.0))
        p.op("vector", lambda e: e.memset(onescol[:], 1.0))
        with ExitStack() as es2:
            def sb2(name, shape, dt=F32):
                return es2.enter_context(nc.sbuf_tensor("s_" + name, shape, dt))
            nwf = sb2("nwf", [128, 8192])
            ynb = sb2("ynb", [128, 8192], BF16)
            ynT = sb2("ynT", [128, 64 * 512], BF16)
            with ExitStack() as esb:
                psN = esb.enter_context(nc.psum_tensor("psN", [128, 512], F32))
                p.dma(stage[0:1, :], nw_d)
                for cc in range(16):
                    p.op("tensor", lambda e, cc=cc:
                         e.matmul(psN[:], ones1[:], stage[0:1, cc * 512:(cc + 1) * 512], start=True, stop=True))
                    p.op("scalar", lambda e, cc=cc: e.copy(nwf[:, cc * 512:(cc + 1) * 512], psN[:]))
            with ExitStack() as es3:
                pt = es3.enter_context(nc.psum_tensor("pt", [128, 128], BF16))
                pO = [es3.enter_context(nc.psum_tensor(f"pO{t}", [128, 512], F32)) for t in range(4)]
                for half in range(2):
                    for tt in range(4):
                        t = half * 4 + tt
                        p.dma(stage[:], y2[t * 128:(t + 1) * 128, :])
                        p.op("scalar", lambda e: e.activation(scr[:], stage[:], AF.Square))
                        p.op("vector", lambda e: e.reduce_sum(st[:, 0:1], scr[:], axis=AX.X))
                        p.op("vector", lambda e: e.tensor_scalar_mul(st[:, 1:2], st[:, 0:1], 1.0 / 8192))
                        p.op("scalar", lambda e: e.activation(st[:, 2:3], st[:, 1:2], AF.Ln, bias=cst[:, 0:1]))
                        p.op("scalar", lambda e: e.activation(st[:, 3:4], st[:, 2:3], AF.Exp, scale=-0.5))
                        p.op("vector", lambda e: e.tensor_scalar_mul(scr[:], stage[:], st[:, 3:4]))
                        p.op("vector", lambda e: e.tensor_mul(ynb[:], scr[:], nwf[:]))
                        for i in range(64):
                            tr(p, ynT[:, i * 512 + tt * 128: i * 512 + tt * 128 + 128],
                               ynb[:, i * 128:(i + 1) * 128], identb[:], pt[:])
                    for c in range(8):
                        for i in range(64):
                            p.dma(wtile[:], opT[i * 128:(i + 1) * 128, c * 512:(c + 1) * 512])
                            p.op("vector", lambda e: e.tensor_copy(wbb[:], wtile[:]))
                            for tt in range(4):
                                p.op("tensor", lambda e, i=i, tt=tt:
                                     e.matmul(pO[tt][:], ynT[:, i * 512 + tt * 128: i * 512 + tt * 128 + 128],
                                              wbb[:], start=(i == 0), stop=(i == 63)))
                        for tt in range(4):
                            t = half * 4 + tt
                            p.op("scalar", lambda e, tt=tt: e.copy(stage[:, 0:512], pO[tt][:]))
                            p.dma(o_scr[t * 128:(t + 1) * 128, c * 512:(c + 1) * 512], stage[:, 0:512])
        h3T = sb("h3T", [128, 32 * 1024], BF16)
        lvacc = sb("lvacc", [128, 8])
        with ExitStack() as es4:
            def sb4(name, shape, dt=F32):
                return es4.enter_context(nc.sbuf_tensor("s_" + name, shape, dt))
            hm3 = sb4("hm3", [128, 4096])
            x2t = sb4("x2t", [128, 4096])
            h3b = sb4("h3b", [128, 4096], BF16)
            pt2 = es4.enter_context(nc.psum_tensor("pt2", [128, 128], BF16))
            pvs = es4.enter_context(nc.psum_tensor("pvs", [1, 1], F32))
            for t in range(8):
                p.dma(x2t[:], x2[t * 128:(t + 1) * 128, :])
                p.dma(stage[:, 0:4096], o_scr[t * 128:(t + 1) * 128, :])
                ln_tile(p, hm3[:], x2t[:], 4096, scr, st, cst)
                p.op("vector", lambda e: e.tensor_add(stage[:, 0:4096], stage[:, 0:4096], hm3[:]))
                ln_tile(p, stage[:, 0:4096], stage[:, 0:4096], 4096, scr, st, cst)
                p.op("vector", lambda e: e.tensor_add(x2t[:], x2t[:], stage[:, 0:4096]))
                p.dma(x3[t * 128:(t + 1) * 128, :], x2t[:])
                ln_tile(p, h3b[:], x2t[:], 4096, scr, st, cst)
                p.op("vector", lambda e: e.tensor_mul(st[:, 6:7], st[:, 5:6], st[:, 5:6]))
                p.op("vector", lambda e: e.tensor_mul(st[:, 6:7], st[:, 6:7], st[:, 4:5]))
                p.op("scalar", lambda e, t=t: e.activation(lvacc[:, t:t + 1], st[:, 6:7], AF.Ln, bias=cst[:, 1:2]))
                p.dma(lv_o[t * 128:(t + 1) * 128, :], lvacc[:, t:t + 1])
                for i in range(32):
                    tr(p, h3T[:, i * 1024 + t * 128: i * 1024 + t * 128 + 128],
                       h3b[:, i * 128:(i + 1) * 128], identb[:], pt2[:])
            p.op("vector", lambda e: e.reduce_sum(st[:, 7:8], lvacc[:], axis=AX.X))
            p.op("tensor", lambda e: e.matmul(pvs[:], st[:, 7:8], onescol[:], start=True, stop=True))
            p.op("scalar", lambda e: e.copy(st[0:1, 0:1], pvs[:]))
            p.dma(lvs_o[:], st[0:1, 0:1])
        with ExitStack() as es6:
            def sb6(name, shape, dt=F32):
                return es6.enter_context(nc.sbuf_tensor("s_" + name, shape, dt))
            avb = sb6("avb", [128, 1024], BF16)
            wv1 = sb6("wv1", [128, 256], BF16)
            T1T = sb6("T1T", [128, 2 * 1024], BF16)
            wvu = sb6("wvu", [128, 512], BF16)
            t2o = sb6("t2o", [128, 512], BF16)
            pT1 = [es6.enter_context(nc.psum_tensor(f"pT1{m}", [128, 1024], F32)) for m in range(2)]
            for i in range(32):
                p.dma(stage[:, 0:1024], avT[i * 128:(i + 1) * 128, :])
                p.op("vector", lambda e: e.tensor_copy(avb[:], stage[:, 0:1024]))
                p.dma(stage[:, 4096:4352], Wvdt[i * 128:(i + 1) * 128, :])
                p.op("vector", lambda e: e.tensor_copy(wv1[:], stage[:, 4096:4352]))
                for m in range(2):
                    for nk in range(2):
                        p.op("tensor", lambda e, i=i, m=m, nk=nk:
                             e.matmul(pT1[m][:, nk * 512:(nk + 1) * 512], wv1[:, m * 128:(m + 1) * 128],
                                      avb[:, nk * 512:(nk + 1) * 512], start=(i == 0), stop=(i == 31)))
            for m in range(2):
                p.op("vector", lambda e, m=m: e.tensor_copy(T1T[:, m * 1024:(m + 1) * 1024], pT1[m][:]))
            for s in range(8):
                for c in range(8):
                    for m in range(2):
                        p.dma(stage[:, 4096:4608], Wvut[m * 128:(m + 1) * 128, c * 512:(c + 1) * 512])
                        p.op("vector", lambda e: e.tensor_copy(wvu[:], stage[:, 4096:4608]))
                        p.op("tensor", lambda e, s=s, m=m:
                             e.matmul(pT1[0][:, 0:512], T1T[:, m * 1024 + s * 128: m * 1024 + s * 128 + 128],
                                      wvu[:], start=(m == 0), stop=(m == 1)))
                    p.op("vector", lambda e: e.tensor_copy(t2o[:], pT1[0][:, 0:512]))
                    p.dma(T2s[s * 128:(s + 1) * 128, c * 512:(c + 1) * 512], t2o[:])
        with ExitStack() as es5:
            def sb5(name, shape, dt=F32):
                return es5.enter_context(nc.sbuf_tensor("s_" + name, shape, dt))
            qTb = sb5("qTb", [64, 1024], BF16)
            rqT = sb5("rqT", [48, 1024], BF16)
            rkT = sb5("rkT", [48, 1024], BF16)
            akb = sb5("akb", [64, 1024], BF16)
            wrb = sb5("wrb", [64, 48], BF16)
            wqb = sb5("wqb", [128, 64], BF16)
            lrel_bc = sb5("lrel_bc", [128, 1024])
            bu1_bc = sb5("bu1_bc", [128, 1024])
            wu2_bc = sb5("wu2_bc", [128, 1024])
            bu2_bc = sb5("bu2_bc", [128, 1])
            sR = sb5("sR", [128, 1024])
            sA = sb5("sA", [128, 1024])
            wk_ = sb5("wk_", [128, 1024])
            gecol = sb5("gecol", [128, 1024])
            wb16 = sb5("wb16", [128, 1024], BF16)
            wT_ = sb5("wT_", [128, 1024], BF16)
            wu1b = sb5("wu1b", [128, 512], BF16)
            g1 = sb5("g1", [128, 1024])
            with ExitStack() as esA:
                psq = esA.enter_context(nc.psum_tensor("psq", [64, 1024], F32))
                psr = esA.enter_context(nc.psum_tensor("psr", [48, 1024], F32))
                psb = esA.enter_context(nc.psum_tensor("psb", [128, 512], F32))
                p.dma(stage[0:1, 0:1024], lrel_d)
                for nk in range(2):
                    p.op("tensor", lambda e, nk=nk:
                         e.matmul(psb[:], ones1[:], stage[0:1, nk * 512:(nk + 1) * 512], start=True, stop=True))
                    p.op("scalar", lambda e, nk=nk: e.copy(lrel_bc[:, nk * 512:(nk + 1) * 512], psb[:]))
                p.dma(stage[0:1, 0:1024], bu1_d)
                for nk in range(2):
                    p.op("tensor", lambda e, nk=nk:
                         e.matmul(psb[:], ones1[:], stage[0:1, nk * 512:(nk + 1) * 512], start=True, stop=True))
                    p.op("scalar", lambda e, nk=nk: e.copy(bu1_bc[:, nk * 512:(nk + 1) * 512], psb[:]))
                p.dma(stage[0:1, 0:1024], Wu2_d)
                for nk in range(2):
                    p.op("tensor", lambda e, nk=nk:
                         e.matmul(psb[:], ones1[:], stage[0:1, nk * 512:(nk + 1) * 512], start=True, stop=True))
                    p.op("scalar", lambda e, nk=nk: e.copy(wu2_bc[:, nk * 512:(nk + 1) * 512], psb[:]))
                p.dma(stage[0:1, 0:1], bu2_d)
                p.op("tensor", lambda e: e.matmul(psb[:, 0:1], ones1[:], stage[0:1, 0:1], start=True, stop=True))
                p.op("scalar", lambda e: e.copy(bu2_bc[:], psb[:, 0:1]))
                for i in range(32):
                    p.dma(stage[:, 0:64], Wqt[i * 128:(i + 1) * 128, :])
                    p.op("vector", lambda e: e.tensor_copy(wqb[:], stage[:, 0:64]))
                    for nk in range(2):
                        p.op("tensor", lambda e, i=i, nk=nk:
                             e.matmul(psq[:, nk * 512:(nk + 1) * 512], wqb[:],
                                      h3T[:, i * 1024 + nk * 512: i * 1024 + (nk + 1) * 512],
                                      start=(i == 0), stop=(i == 31)))
                p.op("vector", lambda e: e.tensor_copy(qTb[:], psq[:]))
                p.dma(stage[0:64, 0:1024], akT)
                p.op("vector", lambda e: e.tensor_copy(akb[:], stage[0:64, 0:1024]))
                p.dma(stage[0:64, 4096:4144], Wrt)
                p.op("vector", lambda e: e.tensor_copy(wrb[:], stage[0:64, 4096:4144]))
                for nk in range(2):
                    p.op("tensor", lambda e, nk=nk:
                         e.matmul(psr[:, nk * 512:(nk + 1) * 512], wrb[:],
                                  akb[:, nk * 512:(nk + 1) * 512], start=True, stop=True))
                p.op("vector", lambda e: e.tensor_copy(rkT[:], psr[:]))
                for nk in range(2):
                    p.op("tensor", lambda e, nk=nk:
                         e.matmul(psr[:, nk * 512:(nk + 1) * 512], wrb[:],
                                  qTb[:, nk * 512:(nk + 1) * 512], start=True, stop=True))
                p.op("vector", lambda e: e.tensor_copy(rqT[:], psr[:]))
            with ExitStack() as esB:
                pss = esB.enter_context(nc.psum_tensor("pss", [128, 1024], F32))
                ptq = esB.enter_context(nc.psum_tensor("ptq", [128, 128], BF16))
                pgA = esB.enter_context(nc.psum_tensor("pgA", [128, 512], F32))
                pgB = esB.enter_context(nc.psum_tensor("pgB", [128, 512], F32))
                for t in range(8):
                    for nk in range(2):
                        p.op("tensor", lambda e, t=t, nk=nk:
                             e.matmul(pss[:, nk * 512:(nk + 1) * 512], rqT[:, t * 128:(t + 1) * 128],
                                      rkT[:, nk * 512:(nk + 1) * 512], start=True, stop=True))
                    p.op("vector", lambda e: e.tensor_scalar_mul(sR[:], pss[:], 1.0 / math.sqrt(48.0)))
                    p.op("vector", lambda e: e.tensor_add(sR[:], sR[:], lrel_bc[:]))
                    for nk in range(2):
                        p.op("tensor", lambda e, t=t, nk=nk:
                             e.matmul(pss[:, nk * 512:(nk + 1) * 512], qTb[:, t * 128:(t + 1) * 128],
                                      akb[:, nk * 512:(nk + 1) * 512], start=True, stop=True))
                    p.op("vector", lambda e: e.tensor_scalar_mul(sA[:], pss[:], 1.0 / 8.0))
                    p.op("vector", lambda e: e.tensor_add(sA[:], sA[:], lrel_bc[:]))
                    p.op("vector", lambda e: e.tensor_copy(wk_[:], sR[:]))
                    for it in range(7):
                        p.op("vector", lambda e: e.reduce_max(st[:, 0:1], wk_[:], axis=AX.X))
                        p.op("vector", lambda e: e.tensor_scalar(gecol[:], wk_[:], st[:, 0:1], None, ALU.is_ge))
                        p.op("vector", lambda e: e.scalar_tensor_tensor(wk_[:], gecol[:], NEG, wk_[:], ALU.mult, ALU.add))
                    p.op("vector", lambda e: e.reduce_max(st[:, 0:1], wk_[:], axis=AX.X))
                    p.op("vector", lambda e: e.tensor_scalar(gecol[:], sR[:], st[:, 0:1], 1.0, ALU.is_ge, ALU.subtract))
                    p.op("vector", lambda e: e.scalar_tensor_tensor(sA[:], gecol[:], -NEG, sA[:], ALU.mult, ALU.add))
                    p.op("vector", lambda e: e.reduce_max(st[:, 0:1], sA[:], axis=AX.X))
                    p.op("vector", lambda e: e.tensor_scalar_mul(st[:, 1:2], st[:, 0:1], -1.0))
                    p.op("scalar", lambda e: e.activation(sA[:], sA[:], AF.Exp, bias=st[:, 1:2]))
                    p.op("vector", lambda e: e.reduce_sum(st[:, 2:3], sA[:], axis=AX.X))
                    p.op("vector", lambda e: e.reciprocal(st[:, 3:4], st[:, 2:3]))
                    p.op("vector", lambda e: e.tensor_scalar_mul(wb16[:], sA[:], st[:, 3:4]))
                    for k in range(8):
                        tr(p, wT_[:, k * 128:(k + 1) * 128], wb16[:, k * 128:(k + 1) * 128], identb[:], ptq[:])
                    for c in range(8):
                        for k in range(8):
                            p.dma(wbb[:], T2s[k * 128:(k + 1) * 128, c * 512:(c + 1) * 512])
                            p.op("tensor", lambda e, k=k:
                                 e.matmul(pgA[:], wT_[:, k * 128:(k + 1) * 128], wbb[:],
                                          start=(k == 0), stop=(k == 7)))
                        p.op("scalar", lambda e: e.copy(stage[:, 0:512], pgA[:]))
                        p.dma(aux[t * 128:(t + 1) * 128, c * 512:(c + 1) * 512], stage[:, 0:512])
                    for nk in range(2):
                        for i in range(32):
                            p.dma(stage[:, 4096:4608], Wu1t[i * 128:(i + 1) * 128, nk * 512:(nk + 1) * 512])
                            p.op("vector", lambda e: e.tensor_copy(wu1b[:], stage[:, 4096:4608]))
                            p.op("tensor", lambda e, i=i, t=t, nk=nk:
                                 e.matmul(pgB[:], h3T[:, i * 1024 + t * 128: i * 1024 + t * 128 + 128],
                                          wu1b[:], start=(i == 0), stop=(i == 31)))
                        p.op("vector", lambda e, nk=nk:
                             e.tensor_add(g1[:, nk * 512:(nk + 1) * 512], pgB[:], bu1_bc[:, nk * 512:(nk + 1) * 512]))
                    p.op("scalar", lambda e: e.activation(g1[:], g1[:], AF.Gelu))
                    p.op("vector", lambda e: e.tensor_mul(g1[:], g1[:], wu2_bc[:]))
                    p.op("vector", lambda e: e.reduce_sum(st[:, 0:1], g1[:], axis=AX.X))
                    p.op("vector", lambda e: e.tensor_add(st[:, 0:1], st[:, 0:1], bu2_bc[:]))
                    p.op("scalar", lambda e: e.activation(st[:, 1:2], st[:, 0:1], AF.Sigmoid))
                    p.dma(sl_o[t * 128:(t + 1) * 128, :], st[:, 1:2])
        p.build()
    return nc


def build_L5():
    nc = bass.Bass("TRN2", target_bir_lowering=False)
    x3 = nc.dram_tensor("x3", [1024, 4096], F32, kind="ExternalInput").ap()
    aux = nc.dram_tensor("aux", [1024, 4096], F32, kind="ExternalInput").ap()
    sl_d = nc.dram_tensor("sl", [1024, 1], F32, kind="ExternalInput").ap()
    lv_d = nc.dram_tensor("lv", [1024, 1], F32, kind="ExternalInput").ap()
    lvs_d = nc.dram_tensor("lvs", [8, 1], F32, kind="ExternalInput").ap()
    gw_d = nc.dram_tensor("gw", [1, 1], F32, kind="ExternalInput").ap()
    gb_d = nc.dram_tensor("gb", [1, 1], F32, kind="ExternalInput").ap()
    yout = nc.dram_tensor("yout", [1024, 4096], F32, kind="ExternalOutput").ap()

    with ExitStack() as es:
        def sb(name, shape, dt=F32):
            return es.enter_context(nc.sbuf_tensor("s_" + name, shape, dt))
        st = sb("st", [128, 8])
        scr = sb("scr", [128, 4096])
        stage = sb("stage", [128, 4096])
        x3t = sb("x3t", [128, 4096])
        hm = sb("hm", [128, 4096])
        ones1 = sb("ones1", [1, 128])
        ones8 = sb("ones8", [8, 1])
        lvs = sb("lvs", [8, 1])
        rbc = sb("rbc", [128, 3])
        cols = sb("cols", [128, 6])
        cst = sb("cst", [128, 2])
        p = Prog(nc)
        p.op("vector", lambda e: e.memset(cst[:, 0:1], EPS))
        p.op("vector", lambda e: e.memset(cst[:, 1:2], 1.0))
        p.op("vector", lambda e: e.memset(ones1[:], 1.0))
        p.op("vector", lambda e: e.memset(ones8[:], 1.0))
        p.dma(lvs[:], lvs_d)
        with ExitStack() as es2:
            ps1 = es2.enter_context(nc.psum_tensor("ps1", [1, 1], F32))
            psb = es2.enter_context(nc.psum_tensor("psb", [128, 1], F32))
            p.op("tensor", lambda e: e.matmul(ps1[:], ones8[:], lvs[:], start=True, stop=True))
            p.op("scalar", lambda e: e.copy(st[0:1, 0:1], ps1[:]))
            p.op("vector", lambda e: e.tensor_scalar(st[0:1, 1:2], st[0:1, 0:1], 1.0 / 8192.0, 1e-6, ALU.mult, ALU.add))
            p.op("vector", lambda e: e.reciprocal(st[0:1, 2:3], st[0:1, 1:2]))
            p.op("tensor", lambda e: e.matmul(psb[:], ones1[:], st[0:1, 2:3], start=True, stop=True))
            p.op("scalar", lambda e: e.copy(rbc[:, 0:1], psb[:]))
            p.dma(st[0:1, 3:4], gw_d)
            p.op("tensor", lambda e: e.matmul(psb[:], ones1[:], st[0:1, 3:4], start=True, stop=True))
            p.op("scalar", lambda e: e.copy(rbc[:, 1:2], psb[:]))
            p.dma(st[0:1, 4:5], gb_d)
            p.op("tensor", lambda e: e.matmul(psb[:], ones1[:], st[0:1, 4:5], start=True, stop=True))
            p.op("scalar", lambda e: e.copy(rbc[:, 2:3], psb[:]))
            for t in range(8):
                p.dma(cols[:, 0:1], lv_d[t * 128:(t + 1) * 128, :])
                p.dma(cols[:, 1:2], sl_d[t * 128:(t + 1) * 128, :])
                p.op("vector", lambda e: e.tensor_mul(cols[:, 2:3], cols[:, 0:1], rbc[:, 0:1]))
                p.op("vector", lambda e: e.tensor_scalar_mul(cols[:, 2:3], cols[:, 2:3], 0.5))
                p.op("vector", lambda e: e.scalar_tensor_tensor(cols[:, 3:4], cols[:, 1:2], 2.5, cols[:, 2:3], ALU.mult, ALU.add))
                p.op("vector", lambda e: e.tensor_scalar(cols[:, 3:4], cols[:, 3:4], 0.0, 5.0, ALU.max, ALU.min))
                p.op("vector", lambda e: e.tensor_scalar(cols[:, 4:5], cols[:, 3:4], 0.5, 1.0 / 1.5, ALU.subtract, ALU.mult))
                p.op("vector", lambda e: e.tensor_scalar(cols[:, 4:5], cols[:, 4:5], 0.0, 1.0, ALU.max, ALU.min))
                p.op("vector", lambda e: e.tensor_scalar(cols[:, 5:6], cols[:, 4:5], rbc[:, 1:2], None, ALU.mult))
                p.op("vector", lambda e: e.tensor_tensor(cols[:, 5:6], cols[:, 5:6], rbc[:, 2:3], ALU.add))
                p.op("scalar", lambda e: e.activation(cols[:, 5:6], cols[:, 5:6], AF.Sigmoid))
                p.op("vector", lambda e: e.tensor_scalar(cols[:, 4:5], cols[:, 5:6], 0.05, None, ALU.is_ge))
                p.op("vector", lambda e: e.tensor_mul(cols[:, 5:6], cols[:, 5:6], cols[:, 4:5]))
                p.dma(x3t[:], x3[t * 128:(t + 1) * 128, :])
                p.dma(stage[:], aux[t * 128:(t + 1) * 128, :])
                ln_tile(p, hm[:], x3t[:], 4096, scr, st, cst)
                p.op("vector", lambda e: e.scalar_tensor_tensor(stage[:], stage[:], cols[:, 5:6], hm[:], ALU.mult, ALU.add))
                ln_tile(p, stage[:], stage[:], 4096, scr, st, cst)
                p.op("vector", lambda e: e.tensor_add(x3t[:], x3t[:], stage[:]))
                p.dma(yout[t * 128:(t + 1) * 128, :], x3t[:])
        p.build()
    return nc


def _run(nc, maps):
    return run_bass_kernel_spmd(nc, maps, core_ids=list(range(8))).results


def kernel(**inputs):
    global LAST_DEBUG
    LAST_DEBUG = {}
    f32 = np.float32
    x = np.ascontiguousarray(np.asarray(inputs["L0_attn_x"], f32))
    upto = int(os.environ.get("ATHENA_UPTO", "99"))

    identb_np = np.eye(128, dtype=BF)
    identf_np = np.eye(128, dtype=f32)
    cumT_np = np.ascontiguousarray(np.triu(np.ones((128, 128), f32)))
    maskA_np = np.where(np.arange(256)[None, :] >= np.arange(128)[:, None], 0.0, NEG).astype(f32)
    E_np = np.ascontiguousarray(np.repeat(np.eye(16, dtype=f32), 16, axis=1))

    cores = [(c // 4, (c % 4) * 1024) for c in range(8)]

    wq = np.ascontiguousarray(np.asarray(inputs["L0_attn_wq"], f32))
    wk = np.ascontiguousarray(np.asarray(inputs["L0_attn_wk"], f32))
    wv = np.ascontiguousarray(np.asarray(inputs["L0_attn_wv"], f32))
    nc1 = build_L1()
    maps = [{"x": np.ascontiguousarray(x[b, q0:q0 + 1024]), "wq": wq, "wk": wk, "wv": wv,
             "identb": identb_np} for (b, q0) in cores]
    r1 = _run(nc1, maps)
    kT_b = [np.concatenate([r1[c]["kT"] for c in range(b * 4, b * 4 + 4)], axis=1) for b in range(2)]
    v_b = [np.concatenate([r1[c]["v"] for c in range(b * 4, b * 4 + 4)], axis=0) for b in range(2)]
    LAST_DEBUG["qT"] = [r1[c]["qT"] for c in range(8)]
    LAST_DEBUG["kT_b"] = kT_b
    LAST_DEBUG["v_b"] = v_b
    if upto <= 1:
        return x

    wo = np.ascontiguousarray(np.asarray(inputs["L0_attn_wo"], f32))
    WT = np.ascontiguousarray(np.asarray(inputs["L0_ffn_W"], f32).T)
    bias = np.ascontiguousarray(np.asarray(inputs["L0_ffn_bias"], f32).reshape(1, 4096))
    AcatT = np.ascontiguousarray(np.asarray(inputs["L0_ffn_A"], f32).transpose(2, 0, 1).reshape(4096, 256))
    Bcat2 = np.ascontiguousarray(np.asarray(inputs["L0_ffn_Bf"], f32).transpose(0, 2, 1).reshape(256, 4096))
    cf = np.ascontiguousarray(np.asarray(inputs["L0_ffn_coeffs"], f32).reshape(16, 1))
    sc = np.ascontiguousarray(np.asarray(inputs["L0_ffn_scales"], f32).reshape(16, 1))
    nc2a = build_L2a()
    maps = [{"x": np.ascontiguousarray(x[b, q0:q0 + 1024]), "qT": r1[c]["qT"], "kTb": kT_b[b],
             "vb": v_b[b], "wo": wo, "WT": WT, "bias": bias, "AcatT": AcatT, "Bcat2": Bcat2,
             "cf": cf, "sc": sc, "E": E_np, "identb": identb_np}
            for c, (b, q0) in enumerate(cores)]
    r2a = _run(nc2a, maps)
    x2 = np.stack([np.concatenate([r2a[b * 4 + q]["x2"] for q in range(4)], axis=0) for b in range(2)])
    LAST_DEBUG["x2"] = x2
    if upto <= 2:
        return x

    inT = np.ascontiguousarray(np.asarray(inputs["L1_attn_in_proj_w"], f32).T)
    convw = np.ascontiguousarray(np.asarray(inputs["L1_attn_conv_w"], f32))
    convb = np.ascontiguousarray(np.asarray(inputs["L1_attn_conv_b"], f32).reshape(10240, 1))
    nc2b = build_L2b()
    maps = []
    for c, (b, q0) in enumerate(cores):
        x2p = np.zeros((1152, 4096), f32)
        lo = q0 - 3
        hi = min(lo + 1152, 4096)
        src_lo = max(lo, 0)
        x2p[src_lo - lo: hi - lo] = x2[b, src_lo:hi]
        maps.append({"x2p": x2p, "inT": inT, "convw": convw, "convb": convb, "identb": identb_np})
    r2b = _run(nc2b, maps)
    xbcT_b = [np.concatenate([r2b[b * 4 + q]["xbcT"] for q in range(4)], axis=1) for b in range(2)]
    z_b = [np.concatenate([r2b[b * 4 + q]["z"] for q in range(4)], axis=0) for b in range(2)]
    dtr_b = [np.concatenate([r2b[b * 4 + q]["dtr"] for q in range(4)], axis=0) for b in range(2)]
    LAST_DEBUG["xbcT_b"] = xbcT_b
    LAST_DEBUG["z_b"] = z_b
    LAST_DEBUG["dtr_b"] = dtr_b
    if upto <= 3:
        return x

    dtb = np.asarray(inputs["L1_attn_dt_bias"], f32)
    alog = np.asarray(inputs["L1_attn_A_log"], f32)
    Dh = np.asarray(inputs["L1_attn_D"], f32)
    nc3 = build_L3()
    maps = []
    for c in range(8):
        b, q = c // 4, c % 4
        hb0 = 32 * q
        maps.append({
            "xT": np.ascontiguousarray(xbcT_b[b][hb0 * 64: hb0 * 64 + 2048]),
            "BT": np.ascontiguousarray(xbcT_b[b][8192 + 128 * (2 * q): 8192 + 128 * (2 * q + 2)]),
            "CT": np.ascontiguousarray(xbcT_b[b][9216 + 128 * (2 * q): 9216 + 128 * (2 * q + 2)]),
            "z": np.ascontiguousarray(z_b[b][:, hb0 * 64: hb0 * 64 + 2048]),
            "dtr": np.ascontiguousarray(dtr_b[b][:, hb0: hb0 + 32]),
            "dtb": np.ascontiguousarray(dtb[hb0:hb0 + 32].reshape(1, 32)),
            "alog": np.ascontiguousarray(alog[hb0:hb0 + 32].reshape(1, 32)),
            "Dh": np.ascontiguousarray(Dh[hb0:hb0 + 32].reshape(1, 32)),
            "cumT": cumT_np, "maskA": maskA_np, "identf": identf_np, "identb": identb_np,
        })
    r3 = _run(nc3, maps)
    y2_b = [np.concatenate([r3[b * 4 + q]["y2"] for q in range(4)], axis=1) for b in range(2)]
    LAST_DEBUG["y2_b"] = y2_b
    if upto <= 4:
        return x

    nw = np.ascontiguousarray(np.asarray(inputs["L1_attn_norm_w"], f32).reshape(1, 8192))
    opT = np.ascontiguousarray(np.asarray(inputs["L1_attn_out_proj_w"], f32).T)
    Wqt = np.ascontiguousarray(np.asarray(inputs["L1_ffn_Wq"], f32).T)
    Wrt = np.ascontiguousarray(np.asarray(inputs["L1_ffn_Wr"], f32).T)
    akT = np.ascontiguousarray(np.asarray(inputs["L1_ffn_aux_keys"], f32).T)
    lrel = np.ascontiguousarray(np.asarray(inputs["L1_ffn_log_rel"], f32).reshape(1, 1024))
    avT = np.ascontiguousarray(np.asarray(inputs["L1_ffn_aux_values"], f32).T)
    Wvdt = np.ascontiguousarray(np.asarray(inputs["L1_ffn_Wvd"], f32).T)
    Wvut = np.ascontiguousarray(np.asarray(inputs["L1_ffn_Wvu"], f32).T)
    Wu1t = np.ascontiguousarray(np.asarray(inputs["L1_ffn_Wu1"], f32).T)
    bu1 = np.ascontiguousarray(np.asarray(inputs["L1_ffn_bu1"], f32).reshape(1, 1024))
    Wu2 = np.ascontiguousarray(np.asarray(inputs["L1_ffn_Wu2"], f32).reshape(1, 1024))
    bu2 = np.ascontiguousarray(np.asarray(inputs["L1_ffn_bu2"], f32).reshape(1, 1))
    nc4 = build_L4()
    maps = [{"x2": np.ascontiguousarray(x2[b, q0:q0 + 1024]),
             "y2": np.ascontiguousarray(y2_b[b][q0:q0 + 1024]),
             "nw": nw, "opT": opT, "Wqt": Wqt, "Wrt": Wrt, "akT": akT, "lrel": lrel,
             "avT": avT, "Wvdt": Wvdt, "Wvut": Wvut, "Wu1t": Wu1t, "bu1": bu1,
             "Wu2": Wu2, "bu2": bu2, "identb": identb_np}
            for c, (b, q0) in enumerate(cores)]
    r4 = _run(nc4, maps)
    LAST_DEBUG["r4"] = r4
    if upto <= 5:
        return x

    lvs = np.ascontiguousarray(np.concatenate([r4[c]["lvs"] for c in range(8)], axis=0))
    gw = np.ascontiguousarray(np.asarray(inputs["L1_ffn_gate_w1"], f32).reshape(1, 1))
    gb = np.ascontiguousarray(np.asarray(inputs["L1_ffn_gate_bias"], f32).reshape(1, 1))
    nc5 = build_L5()
    maps = [{"x3": r4[c]["x3"], "aux": r4[c]["aux"], "sl": r4[c]["sl"], "lv": r4[c]["lv"],
             "lvs": lvs, "gw": gw, "gb": gb} for c in range(8)]
    r5 = _run(nc5, maps)
    out = np.stack([np.concatenate([r5[b * 4 + q]["yout"] for q in range(4)], axis=0) for b in range(2)])
    return out.astype(np.float32)
